# version 01
# baseline (speedup 1.0000x reference)
import math
import numpy as np
import ml_dtypes
from contextlib import ExitStack

import concourse.bass as bass
import concourse.mybir as mybir
from concourse.bass_utils import run_bass_kernel_spmd

F32 = mybir.dt.float32
BF16 = mybir.dt.bfloat16
I32 = mybir.dt.int32
AF = mybir.ActivationFunctionType
ALU = mybir.AluOpType
AX = mybir.AxisListType

B, T, D = 4, 4096, 1024
NT = 2048
NTILE = NT // 128
HD = 128
E0, EH, CAP = 8, 2048, 5120
SHD = 4096
E1, J1 = 16, 256
EPS = 1e-5
MAGIC = 12582912.0
TWO_PI = 2.0 * math.pi
INV_2PI = 1.0 / TWO_PI

RANGES = {0: [(0, 1024), (3072, 4096)], 1: [(1024, 3072)]}


class Ch:
    def __init__(self, sem):
        self.sem = sem
        self.n = 0

    def add(self, inst, k=1):
        inst.then_inc(self.sem, k)
        self.n += k
        return self.n

    def w(self, eng, v=None):
        eng.wait_ge(self.sem, self.n if v is None else v)



def build_launch_a():
    nc = bass.Bass("TRN2", target_bir_lowering=False)
    xa = nc.dram_tensor("xa", [NT, D], F32, kind="ExternalInput").ap()
    projwt = nc.dram_tensor("projwt", [D, E0], F32, kind="ExternalInput").ap()
    projb = nc.dram_tensor("projb", [1, E0], F32, kind="ExternalInput").ap()
    ebias = nc.dram_tensor("ebias", [1, E0], F32, kind="ExternalInput").ap()
    xnt_o = nc.dram_tensor("xnt", [D, NT], BF16, kind="ExternalOutput").ap()
    w8_o = nc.dram_tensor("w8", [NT, E0], F32, kind="ExternalOutput").ap()

    es = ExitStack()
    sb = lambda n, s, d=F32: es.enter_context(nc.sbuf_tensor(n, list(s), d))
    ps = lambda n, s, d=F32: es.enter_context(nc.psum_tensor(n, list(s), d))
    with es:
        X16 = sb("X16", (128, NTILE, D))
        XNT = sb("XNT", (128, 8, NT), BF16)
        SQ = sb("SQ", (128, 2, D))
        STAT = sb("STAT", (128, 6, NTILE))
        PRJW = sb("PRJW", (128, 8, E0), BF16)
        PBROW = sb("PBROW", (1, E0), BF16)
        EBROW = sb("EBROW", (1, E0))
        EB8 = sb("EB8", (128, E0))
        ONESB = sb("ONESB", (1, 128), BF16)
        ONESF = sb("ONESF", (1, 128))
        IDENT = sb("IDENT", (128, 128))
        CONS = sb("CONS", (128, 2))
        S8 = sb("S8", (128, E0))
        RS = sb("RS", (128, 2))
        TOP8 = sb("TOP8", (128, 8))
        MAXB = sb("MAXB", (128, 8))
        ZAP = sb("ZAP", (128, 8))
        MSK = sb("MSK", (128, 8))
        T1 = sb("T1", (128, 8))
        W8S = sb("W8S", (128, NTILE, E0))
        PT2 = [ps("PT0", (128, 128)), ps("PT1", (128, 128))]
        ZP2 = [ps("ZP0", (128, E0)), ps("ZP1", (128, E0))]
        EBP = ps("EBP", (128, E0))

        with nc.semaphore("dl") as dl_s, nc.semaphore("dlg") as dlg_s, \
             nc.semaphore("cg") as cg_s, nc.semaphore("cv") as cv_s, \
             nc.semaphore("ca") as ca_s, nc.semaphore("tr") as tr_s, \
             nc.semaphore("ev") as ev_s, nc.semaphore("zp") as zp_s, \
             nc.semaphore("sg") as sg_s, nc.semaphore("rc") as rc_s, \
             nc.semaphore("ebs") as ebs_s, nc.semaphore("od") as od_s, \
             nc.Block() as blk:
            dl, dlg, cg, cv, ca = Ch(dl_s), Ch(dlg_s), Ch(cg_s), Ch(cv_s), Ch(ca_s)
            tr, ev, zp, sg, rc, ebs, od = (Ch(tr_s), Ch(ev_s), Ch(zp_s),
                                           Ch(sg_s), Ch(rc_s), Ch(ebs_s), Ch(od_s))

            @blk.sync
            def _(s):
                dl.add(s.dma_start(X16[:], xa.rearrange("(n p) d -> p n d", p=128)), 16)
                dl.add(s.dma_start(EBROW[:], ebias[:, :]), 16)

            @blk.gpsimd
            def _(g):
                dlg.add(g.dma_start(PRJW[:], projwt.rearrange("(c p) e -> p c e", p=128)), 16)
                dlg.add(g.dma_start(PBROW[:], projb[:, :]), 16)
                cg.add(g.memset(IDENT[:], 0.0))
                cg.w(g)
                cg.add(g.affine_select(out=IDENT[:], in_=IDENT[:],
                                       compare_op=ALU.not_equal, fill=1.0, base=0,
                                       pattern=[[-1, 128]], channel_multiplier=1))

            @blk.vector
            def _(v):
                cv.add(v.memset(ONESB[:], 1.0))
                cv.add(v.memset(ONESF[:], 1.0))
                cv.add(v.memset(CONS[:, 0:1], EPS))
                cv.add(v.memset(CONS[:, 1:2], -0.5))
                v.wait_ge(dl_s, 32)
                for n in range(NTILE):
                    cv.add(v.reduce_sum(STAT[:, 0, n:n + 1], X16[:, n, :], AX.X))
                cv.w(v)
                v.wait_ge(ca_s, NTILE)
                cv.add(v.tensor_scalar(STAT[:, 2, :], STAT[:, 0, :], 1.0 / D, None, ALU.mult))
                cv.w(v)
                cv.add(v.tensor_scalar(STAT[:, 3, :], STAT[:, 1, :], 1.0 / D, None, ALU.mult))
                cv.w(v)
                cv.add(v.tensor_tensor(STAT[:, 4, :], STAT[:, 2, :], STAT[:, 2, :], ALU.mult))
                cv.w(v)
                cv.add(v.tensor_tensor(STAT[:, 4, :], STAT[:, 3, :], STAT[:, 4, :], ALU.subtract))
                stats_done = cv.n
                v.wait_ge(ca_s, NTILE + 2)
                for n in range(NTILE):
                    cv.add(v.tensor_scalar(X16[:, n, :], X16[:, n, :],
                                           STAT[:, 2, n:n + 1], STAT[:, 5, n:n + 1],
                                           ALU.subtract, ALU.mult))
                for j in range(NTILE * 8):
                    v.wait_ge(tr_s, j + 1)
                    n, k = divmod(j, 8)
                    ev.add(v.tensor_copy(XNT[:, k, n * 128:(n + 1) * 128], PT2[j % 2][:]))
                v.wait_ge(ebs_s, 1)
                ebs.n = 1
                ebs.add(v.tensor_copy(EB8[:], EBP[:]))
                for n in range(NTILE):
                    v.wait_ge(sg_s, n + 1)
                    if n == 0:
                        v.wait_ge(ebs_s, 2)
                    rc.add(v.tensor_tensor(S8[:], S8[:], EB8[:], ALU.add)); rc.w(v)
                    rc.add(v.tensor_scalar(S8[:], S8[:], 1e-12, None, ALU.max)); rc.w(v)
                    rc.add(v.reduce_sum(RS[:, 0:1], S8[:], AX.X)); rc.w(v)
                    rc.add(v.reciprocal(RS[:, 1:2], RS[:, 0:1])); rc.w(v)
                    rc.add(v.tensor_scalar(S8[:], S8[:], RS[:, 1:2], None, ALU.mult)); rc.w(v)
                    rc.add(v.max(out=TOP8[:], in_=S8[:])); rc.w(v)
                    rc.add(v.memset(MAXB[:], -1e30)); rc.w(v)
                    rc.add(v.tensor_copy(MAXB[:, 0:2], TOP8[:, 0:2])); rc.w(v)
                    rc.add(v.match_replace(out=ZAP[:], in_to_replace=MAXB[:],
                                           in_values=S8[:], imm_value=-5.0)); rc.w(v)
                    rc.add(v.tensor_tensor(MSK[:], S8[:], ZAP[:], ALU.not_equal)); rc.w(v)
                    rc.add(v.tensor_tensor(RS[:, 0:1], TOP8[:, 0:1], TOP8[:, 1:2], ALU.add)); rc.w(v)
                    rc.add(v.reciprocal(RS[:, 1:2], RS[:, 0:1])); rc.w(v)
                    rc.add(v.tensor_scalar(T1[:], S8[:], RS[:, 1:2], 1.0, ALU.mult, ALU.add)); rc.w(v)
                    rc.add(v.tensor_tensor(T1[:], T1[:], MSK[:], ALU.mult)); rc.w(v)
                    rc.add(v.tensor_scalar(W8S[:, n, :], T1[:], 1.0, None, ALU.subtract))

            @blk.scalar
            def _(s):
                s.wait_ge(dl_s, 32)
                for n in range(NTILE):
                    if n >= 2:
                        s.wait_ge(ca_s, n - 1)
                    ca.add(s.activation(SQ[:, n % 2, :], X16[:, n, :], AF.Square,
                                        accum_out=STAT[:, 1, n:n + 1]))
                s.wait_ge(cv_s, 4 + NTILE + 4)
                ca.add(s.activation(STAT[:, 5, :], STAT[:, 4, :], AF.Ln, bias=CONS[:, 0:1]))
                ca.w(s)
                ca.add(s.activation(STAT[:, 5, :], STAT[:, 5, :], AF.Exp, scale=CONS[:, 1:2]))
                for n in range(NTILE):
                    s.wait_ge(zp_s, n + 1)
                    if n >= 1:
                        s.wait_ge(rc_s, 15 * n)
                    sg.add(s.activation(S8[:], ZP2[n % 2][:], AF.Sigmoid))

            @blk.tensor
            def _(t):
                t.wait_ge(cv_s, 2)
                t.wait_ge(dl_s, 32)
                ebs.add(t.matmul(EBP[:], ONESF[:], EBROW[:], start=True, stop=True))
                t.wait_ge(cg_s, 2)
                xn_base = 4 + NTILE + 4
                for j in range(NTILE * 8):
                    n, k = divmod(j, 8)
                    t.wait_ge(cv_s, xn_base + n + 1)
                    if j >= 2:
                        t.wait_ge(ev_s, j - 1)
                    tr.add(t.transpose(PT2[j % 2][:], X16[:, n, k * 128:(k + 1) * 128],
                                       IDENT[:]))
                t.wait_ge(dlg_s, 32)
                for n in range(NTILE):
                    t.wait_ge(ev_s, 8 * (n + 1))
                    if n >= 2:
                        t.wait_ge(sg_s, n - 1)
                    for k in range(8):
                        t.matmul(ZP2[n % 2][:], XNT[:, k, n * 128:(n + 1) * 128],
                                 PRJW[:, k, :], start=(k == 0), stop=False)
                    zp.add(t.matmul(ZP2[n % 2][:], ONESB[:], PBROW[:],
                                    start=False, stop=True))

            @blk.sync
            def _(s):
                s.wait_ge(ev_s, NTILE * 8)
                od.add(s.dma_start(xnt_o.rearrange("(c p) t -> p c t", p=128), XNT[:]), 16)
                s.wait_ge(rc_s, 15 * NTILE)
                od.add(s.dma_start(w8_o.rearrange("(n p) e -> p n e", p=128), W8S[:]), 16)
                s.wait_ge(od_s, 32)
    return nc




NIT_CAP = 26


def build_launch_b(skip=()):
    nc = bass.Bass("TRN2", target_bir_lowering=False)
    d = {"skip": skip}
    d["xb"] = nc.dram_tensor("xb", [NT, D], F32, kind="ExternalInput").ap()
    d["xnts"] = nc.dram_tensor("xnts", [D, NT], BF16, kind="ExternalInput").ap()
    d["xntn"] = nc.dram_tensor("xntn", [D, T], BF16, kind="ExternalInput").ap()
    d["w8ts"] = nc.dram_tensor("w8ts", [128, 1024], F32, kind="ExternalInput").ap()
    d["w8own"] = nc.dram_tensor("w8own", [NT, E0], F32, kind="ExternalInput").ap()
    for nm, shp in [("wqt", [D, HD]), ("wkt", [D, HD]), ("wvt", [D, HD]),
                    ("bq", [HD, 1]), ("bk", [HD, 1]), ("bvr", [1, HD]),
                    ("wl", [HD, 1]), ("pbi", [HD, 1]),
                    ("wot", [HD, D]), ("bo", [1, D]),
                    ("sh1t", [D, SHD]), ("sh1b", [128, SHD // 128]),
                    ("sh2t", [SHD, D]), ("sh2b", [1, D]),
                    ("fc1t", [E0, D, EH]), ("fc1b", [E0, 128, EH // 128]),
                    ("fc2t", [E0, EH, D]), ("fc2ba", [E0, D])]:
        d[nm] = nc.dram_tensor(nm, shp, F32, kind="ExternalInput").ap()
    d["x1"] = nc.dram_tensor("x1", [NT, D], F32, kind="ExternalOutput").ap()
    d["xnt1"] = nc.dram_tensor("xnt1", [D, NT], BF16, kind="ExternalOutput").ap()
    d["thr"] = nc.dram_tensor("thr", [1, E0], F32, kind="ExternalOutput").ap()
    d["thrd"] = nc.dram_tensor("thrd", [1, E0], F32, kind="Internal").ap()

    es = ExitStack()
    d["es"] = es
    sb = lambda n, s, dt=F32: es.enter_context(nc.sbuf_tensor(n, list(s), dt))
    sem = lambda n: es.enter_context(nc.semaphore(n))
    with es:
        d["X16"] = sb("X16", (128, NTILE, D))
        d["XNTS"] = sb("XNTS", (128, 8, NT), BF16)
        d["W8TS"] = sb("W8TS", (128, 1024))
        d["W8OWN"] = sb("W8OWN", (128, NTILE, E0))
        d["G16"] = sb("G16", (128, NTILE, E0))
        d["GTB"] = sb("GTB", (8, NT), BF16)
        d["SELC"] = sb("SELC", (8, E0, 128), BF16)
        d["HIREP"] = sb("HIREP", (128, E0))
        d["ONESB"] = sb("ONESB", (1, 128), BF16)
        d["ONESF"] = sb("ONESF", (1, 128))
        d["IDENT"] = sb("IDENT", (128, 128))
        d["IDENTB"] = sb("IDENTB", (128, 128), BF16)
        d["CONS"] = sb("CONS", (128, 8))
        d["REP8F"] = sb("REP8F", (8, 128))
        d["REPTF"] = sb("REPTF", (128, 8))
        d["STAT"] = sb("STAT", (128, 6, NTILE))
        d["dl_s"] = sem("dl")

        _b0_loads(nc, d)
        if "cap" not in skip:
            _emit_capacity(nc, d)
            _emit_gates(nc, d)
        if "attn" not in skip:
            _emit_attn0(nc, d)
        if "sh" not in skip:
            _emit_shared(nc, d)
        if "moe" not in skip:
            _emit_moe(nc, d)
        _emit_ln1_stores(nc, d)
    return nc


def _b0_loads(nc, d):
    es = d["es"]
    cg0 = es.enter_context(nc.semaphore("cg0"))
    with nc.Block() as blk:
        @blk.sync
        def _(s):
            dl = Ch(d["dl_s"])
            dl.add(s.dma_start(d["X16"][:], d["xb"].rearrange("(n p) dd -> p n dd", p=128)), 16)
            dl.add(s.dma_start(d["XNTS"][:], d["xnts"].rearrange("(c p) t -> p c t", p=128)), 16)
            dl.add(s.dma_start(d["W8TS"][:], d["w8ts"][:, :]), 16)
            dl.add(s.dma_start(d["W8OWN"][:], d["w8own"].rearrange("(n p) e -> p n e", p=128)), 16)

        @blk.gpsimd
        def _(g):
            cg = Ch(cg0)
            cg.add(g.memset(d["IDENT"][:], 0.0))
            cg.add(g.memset(d["IDENTB"][:], 0.0))
            cg.add(g.memset(d["REP8F"][:], 1.0))
            cg.add(g.memset(d["REPTF"][:], 1.0))
            cg.add(g.memset(d["SELC"][:], 1.0))
            g.wait_ge(cg0, 5)
            cg.add(g.affine_select(out=d["IDENT"][:], in_=d["IDENT"][:], compare_op=ALU.not_equal,
                                   fill=1.0, base=0, pattern=[[-1, 128]], channel_multiplier=1))
            cg.add(g.affine_select(out=d["IDENTB"][:], in_=d["IDENTB"][:], compare_op=ALU.not_equal,
                                   fill=1.0, base=0, pattern=[[-1, 128]], channel_multiplier=1))
            cg.add(g.affine_select(out=d["SELC"][:], in_=d["SELC"][:], compare_op=ALU.is_ge,
                                   fill=0.0, base=0, pattern=[[-1, E0], [0, 128]],
                                   channel_multiplier=1))
            cg.add(g.affine_select(out=d["REP8F"][:], in_=d["REP8F"][:], compare_op=ALU.is_ge,
                                   fill=0.0, base=0, pattern=[[1, 128]], channel_multiplier=-16))
            cg.add(g.affine_select(out=d["REPTF"][:], in_=d["REPTF"][:], compare_op=ALU.is_ge,
                                   fill=0.0, base=0, pattern=[[-16, 8]], channel_multiplier=1))
            g.wait_ge(cg0, 10)
            cg.add(g.affine_select(out=d["REP8F"][:], in_=d["REP8F"][:], compare_op=ALU.is_ge,
                                   fill=0.0, base=15, pattern=[[-1, 128]], channel_multiplier=16))
            cg.add(g.affine_select(out=d["REPTF"][:], in_=d["REPTF"][:], compare_op=ALU.is_ge,
                                   fill=0.0, base=15, pattern=[[16, 8]], channel_multiplier=-1))
            cg.add(g.affine_select(out=d["SELC"][:], in_=d["SELC"][:], compare_op=ALU.is_ge,
                                   fill=0.0, base=0, pattern=[[1, E0], [0, 128]],
                                   channel_multiplier=-1))

        @blk.vector
        def _(v):
            v.memset(d["ONESB"][:], 1.0)
            v.memset(d["ONESF"][:], 1.0)
            v.memset(d["CONS"][:, 0:1], EPS)
            v.memset(d["CONS"][:, 1:2], -0.5)
            v.memset(d["CONS"][:, 2:3], 1.0 / math.sqrt(128.0))
            v.memset(d["CONS"][:, 3:4], math.pi / 2.0)


def _emit_capacity(nc, d):
    es = d["es"]
    bt_s = es.enter_context(nc.semaphore("bt"))
    bv_s = es.enter_context(nc.semaphore("bv"))
    bh_s = es.enter_context(nc.semaphore("bh"))
    mid_ready = lambda it: 3 + 9 * it + 2
    cnt_ready = lambda it: 3 + 9 * it + 5
    bv_end = 3 + 9 * NIT_CAP
    with nc.sbuf_tensor("LOA", [8, 1], F32) as LOA, \
         nc.sbuf_tensor("LOB", [8, 1], F32) as LOB, \
         nc.sbuf_tensor("HIA", [8, 1], F32) as HIA, \
         nc.sbuf_tensor("HIB", [8, 1], F32) as HIB, \
         nc.sbuf_tensor("MIDC", [8, 1], F32) as MIDC, \
         nc.sbuf_tensor("MID128", [128, 1], F32) as MID128, \
         nc.sbuf_tensor("CMP", [128, 1024], F32) as CMP, \
         nc.sbuf_tensor("CNTT", [128, 1], F32) as CNTT, \
         nc.sbuf_tensor("CNT8", [8, 1], F32) as CNT8, \
         nc.sbuf_tensor("CAPC", [8, 1], F32) as CAPC, \
         nc.sbuf_tensor("PREDC", [8, 1], mybir.dt.uint32) as PREDC, \
         nc.sbuf_tensor("HIROW", [1, 8], F32) as HIROW, \
         nc.psum_tensor("MIDP", [128, 1], F32) as MIDP, \
         nc.psum_tensor("C8P", [8, 1], F32) as C8P, \
         nc.psum_tensor("HRP", [128, 8], F32) as HRP, \
         nc.Block() as blk:
        LO = [LOA, LOB]
        HI = [HIA, HIB]

        @blk.vector
        def _(v):
            bv = Ch(bv_s)
            bv.add(v.memset(LO[0][:], -0.5))
            bv.add(v.memset(HI[0][:], 1.5))
            bv.add(v.memset(CAPC[:], float(CAP)))
            v.wait_ge(d["dl_s"], 64)
            for it in range(NIT_CAP):
                cur, nxt = it % 2, (it + 1) % 2
                bv.w(v)
                bv.add(v.tensor_tensor(MIDC[:], LO[cur][:], HI[cur][:], ALU.add))
                bv.w(v)
                bv.add(v.tensor_scalar(MIDC[:], MIDC[:], 0.5, None, ALU.mult))
                v.wait_ge(bt_s, 2 * it + 1)
                bv.add(v.tensor_copy(MID128[:], MIDP[:]))
                bv.w(v)
                bv.add(v.tensor_scalar(CMP[:], d["W8TS"][:], MID128[:], None, ALU.is_gt))
                bv.w(v)
                bv.add(v.reduce_sum(CNTT[:, 0:1], CMP[:], AX.X))
                v.wait_ge(bt_s, 2 * it + 2)
                bv.add(v.tensor_copy(CNT8[:], C8P[:]))
                bv.w(v)
                bv.add(v.tensor_scalar(PREDC[:], CNT8[:], CAPC[:], None, ALU.is_gt))
                bv.w(v)
                bv.add(v.select(LO[nxt][:], PREDC[:], MIDC[:], LO[cur][:], add_drain=True))
                bv.w(v)
                bv.add(v.select(HI[nxt][:], PREDC[:], HI[cur][:], MIDC[:], add_drain=True))
            v.wait_ge(bt_s, 2 * NIT_CAP + 1)
            bv.add(v.tensor_copy(d["HIREP"][:], HRP[:]))

        @blk.tensor
        def _(t):
            bt = Ch(bt_s)
            for it in range(NIT_CAP):
                t.wait_ge(bv_s, mid_ready(it))
                bt.add(t.matmul(MIDP[:], d["REP8F"][:], MIDC[:], start=True, stop=True))
                t.wait_ge(bv_s, cnt_ready(it))
                bt.add(t.matmul(C8P[:], d["REPTF"][:], CNTT[:, 0:1], start=True, stop=True))
            t.wait_ge(bh_s, 48)
            bt.add(t.matmul(HRP[:], d["ONESF"][:], HIROW[:], start=True, stop=True))

        @blk.sync
        def _(s):
            bh = Ch(bh_s)
            HIfin = HI[NIT_CAP % 2]
            s.wait_ge(bv_s, bv_end)
            bh.add(s.dma_start(d["thrd"].rearrange("o e -> e o"), HIfin[:]), 16)
            s.wait_ge(bh_s, 16)
            bh.add(s.dma_start(HIROW[:], d["thrd"][0:1, :]), 16)
            bh.add(s.dma_start(d["thr"][0:1, :], d["thrd"][0:1, :]), 16)
            s.wait_ge(bh_s, 48)


def _emit_gates(nc, d):
    es = d["es"]
    gv_s = es.enter_context(nc.semaphore("gv"))
    gt_s = es.enter_context(nc.semaphore("gt"))
    gc_s = es.enter_context(nc.semaphore("gc"))
    with nc.sbuf_tensor("CMPG", [128, 2, E0], F32) as CMPG, \
         nc.psum_tensor("PSGA", [8, 128], F32) as PSGA, \
         nc.psum_tensor("PSGB", [8, 128], F32) as PSGB, \
         nc.Block() as blk:
        PSG = [PSGA, PSGB]

        @blk.vector
        def _(v):
            gv = Ch(gv_s)
            for n in range(NTILE):
                if n >= 2:
                    gv.w(v, 2 * (n - 1))
                gv.add(v.tensor_tensor(CMPG[:, n % 2, :], d["W8OWN"][:, n, :],
                                       d["HIREP"][:], ALU.is_gt))
                gv.w(v)
                gv.add(v.tensor_tensor(d["G16"][:, n, :], d["W8OWN"][:, n, :],
                                       CMPG[:, n % 2, :], ALU.mult))

        @blk.tensor
        def _(t):
            gt = Ch(gt_s)
            for n in range(NTILE):
                t.wait_ge(gv_s, 2 * (n + 1))
                if n >= 2:
                    t.wait_ge(gc_s, n - 1)
                gt.add(t.transpose(PSG[n % 2][:], d["G16"][:, n, :], d["IDENT"][:]))

        @blk.scalar
        def _(s):
            gc = Ch(gc_s)
            for n in range(NTILE):
                s.wait_ge(gt_s, n + 1)
                gc.add(s.copy(d["GTB"][0:8, n * 128:(n + 1) * 128], PSG[n % 2][:]))


def _emit_attn0(nc, d):
    es = d["es"]
    adl = es.enter_context(nc.semaphore("adl"))
    agl = es.enter_context(nc.semaphore("agl"))
    mm_s = es.enter_context(nc.semaphore("amm"))
    tv_s = es.enter_context(nc.semaphore("atv"))
    ta_s = es.enter_context(nc.semaphore("ata"))
    ve_s = es.enter_context(nc.semaphore("ave"))
    sp_s = es.enter_context(nc.semaphore("asp"))
    px_s = es.enter_context(nc.semaphore("apx"))
    pv_s = es.enter_context(nc.semaphore("apv"))
    ep_s = es.enter_context(nc.semaphore("aep"))
    ot_s = es.enter_context(nc.semaphore("aot"))
    ym_s = es.enter_context(nc.semaphore("aym"))
    xa_s = es.enter_context(nc.semaphore("axa"))

    CONS, XNTS, IDENTB, ONESB, X16 = (d["CONS"], d["XNTS"], d["IDENTB"],
                                      d["ONESB"], d["X16"])

    def mm_of_trig(g_i):
        return g_i + 1 if g_i < 8 else 17 + g_i

    def mm_of_v(vt):
        return 9 + vt if vt < 16 else 13 + vt

    aes = ExitStack()
    asb = lambda n, shp, dt=F32: aes.enter_context(nc.sbuf_tensor(n, list(shp), dt))
    a1es = ExitStack()
    a1sb = lambda n, shp, dt=F32: a1es.enter_context(nc.sbuf_tensor(n, list(shp), dt))
    a1ps = lambda n, shp, dt=F32: a1es.enter_context(nc.psum_tensor(n, list(shp), dt))
    with aes:
        WOTS = asb("WOTS", (128, D), BF16)
        BOROW = asb("BOROW", (1, D), BF16)
        CQT = asb("CQT", (128, NT), BF16); SQT = asb("SQT", (128, NT), BF16)
        CKT = asb("CKT", (128, T), BF16); SKT = asb("SKT", (128, T), BF16)
        VV = asb("VV", (128, 32, 130), BF16)
        a1es.__enter__()
        XNTNH = a1sb("XNTNH", (128, 8, 2048), BF16)
        WQT = a1sb("WQT", (128, 8, HD), BF16)
        WKT = a1sb("WKT", (128, 8, HD), BF16)
        WVT = a1sb("WVT", (128, 8, HD), BF16)
        BVROW = a1sb("BVROW", (1, HD), BF16)
        BQ = a1sb("BQ", (128, 1)); BK = a1sb("BK", (128, 1))
        WLS = a1sb("WLS", (128, 1)); PBIS = a1sb("PBIS", (128, 1))
        INVW = a1sb("INVW", (128, 1)); C1Q = a1sb("C1Q", (128, 1)); C1K = a1sb("C1K", (128, 1))
        THB = a1sb("THB", (128, 2, 512)); UB = a1sb("UB", (128, 2, 512))
        RB = a1sb("RB", (128, 2, 512)); W1B = a1sb("W1B", (128, 2, 512))
        W2B = a1sb("W2B", (128, 2, 512))
        TP0 = a1ps("TP0", (128, 512)); TP1 = a1ps("TP1", (128, 512))
        VP0 = a1ps("VP0", (128, HD)); VP1 = a1ps("VP1", (128, HD))
        TP = [TP0, TP1]
        VP = [VP0, VP1]

        with nc.Block() as blk:
            @blk.sync
            def _(s):
                a = Ch(adl)
                a.add(s.dma_start(XNTNH[:], d["xntn"][:, 0:2048]
                                  .rearrange("(c p) t -> p c t", p=128)), 16)
                a.add(s.dma_start(BQ[:], d["bq"][:, :]), 16)
                a.add(s.dma_start(BK[:], d["bk"][:, :]), 16)
                a.add(s.dma_start(WLS[:], d["wl"][:, :]), 16)
                a.add(s.dma_start(PBIS[:], d["pbi"][:, :]), 16)
                s.wait_ge(mm_s, 24)
                a.add(s.dma_start(XNTNH[:], d["xntn"][:, 2048:4096]
                                  .rearrange("(c p) t -> p c t", p=128)), 16)

            @blk.gpsimd
            def _(g):
                a = Ch(agl)
                a.add(g.dma_start(WQT[:], d["wqt"].rearrange("(c p) h -> p c h", p=128)), 16)
                a.add(g.dma_start(WKT[:], d["wkt"].rearrange("(c p) h -> p c h", p=128)), 16)
                a.add(g.dma_start(WVT[:], d["wvt"].rearrange("(c p) h -> p c h", p=128)), 16)
                a.add(g.dma_start(WOTS[:], d["wot"][:, :]), 16)
                a.add(g.dma_start(BVROW[:], d["bvr"][:, :]), 16)
                a.add(g.dma_start(BOROW[:], d["bo"][:, :]), 16)

            @blk.vector
            def _(v):
                tv = Ch(tv_s)
                ve = Ch(ve_s)
                v.wait_ge(adl, 80)
                tv.add(v.memset(VV[:, :, 128:130], 0.0)); tv.w(v)
                tv.add(v.memset(VV[:, :, 128:129], 1.0))
                tv.add(v.tensor_scalar(INVW[:], WLS[:], 1e-8, None, ALU.add)); tv.w(v)
                tv.add(v.reciprocal(INVW[:], INVW[:])); tv.w(v)
                tv.add(v.tensor_tensor(C1Q[:], BQ[:], INVW[:], ALU.mult)); tv.w(v)
                tv.add(v.tensor_tensor(C1Q[:], C1Q[:], PBIS[:], ALU.add)); tv.w(v)
                tv.add(v.tensor_tensor(C1K[:], BK[:], INVW[:], ALU.mult)); tv.w(v)
                tv.add(v.tensor_tensor(C1K[:], C1K[:], PBIS[:], ALU.add))
                TRIG_BASE = tv.n

                def trig_tile(g_i):
                    p = g_i % 2
                    c1 = C1Q if g_i < 4 else C1K
                    if g_i >= 2:
                        v.wait_ge(ta_s, 2 * (g_i - 1))
                    v.wait_ge(mm_s, mm_of_trig(g_i))
                    tv.add(v.tensor_scalar(THB[:, p, :], TP[p][:], INVW[:], c1[:],
                                           ALU.mult, ALU.add)); tv.w(v)
                    tv.add(v.tensor_scalar(UB[:, p, :], THB[:, p, :], INV_2PI, MAGIC,
                                           ALU.mult, ALU.add)); tv.w(v)
                    tv.add(v.tensor_scalar(RB[:, p, :], UB[:, p, :], MAGIC, -TWO_PI,
                                           ALU.subtract, ALU.mult)); tv.w(v)
                    tv.add(v.tensor_tensor(W1B[:, p, :], RB[:, p, :], THB[:, p, :],
                                           ALU.add)); tv.w(v)
                    tv.add(v.tensor_scalar(UB[:, p, :], THB[:, p, :], INV_2PI, 0.25,
                                           ALU.mult, ALU.add)); tv.w(v)
                    tv.add(v.tensor_scalar(UB[:, p, :], UB[:, p, :], MAGIC, MAGIC,
                                           ALU.add, ALU.subtract)); tv.w(v)
                    tv.add(v.tensor_scalar(UB[:, p, :], UB[:, p, :], -TWO_PI, None,
                                           ALU.mult)); tv.w(v)
                    tv.add(v.tensor_tensor(W2B[:, p, :], UB[:, p, :], THB[:, p, :],
                                           ALU.add))

                def v_evac(vt):
                    v.wait_ge(mm_s, mm_of_v(vt))
                    ve.add(v.tensor_copy(VV[:, vt, 0:128], VP[vt % 2][:]))

                for g_i in range(8):
                    trig_tile(g_i)
                for vt in range(16):
                    v_evac(vt)
                for g_i in range(8, 12):
                    trig_tile(g_i)
                for vt in range(16, 32):
                    v_evac(vt)

            @blk.scalar
            def _(s):
                ta = Ch(ta_s)
                for g_i in range(12):
                    p = g_i % 2
                    if g_i < 4:
                        dst_s = SQT[:, g_i * 512:(g_i + 1) * 512]
                        dst_c = CQT[:, g_i * 512:(g_i + 1) * 512]
                    else:
                        dst_s = SKT[:, (g_i - 4) * 512:(g_i - 3) * 512]
                        dst_c = CKT[:, (g_i - 4) * 512:(g_i - 3) * 512]
                    base = 8 + 8 * g_i
                    s.wait_ge(tv_s, base + 4)
                    ta.add(s.activation(dst_s, W1B[:, p, :], AF.Sin))
                    s.wait_ge(tv_s, base + 8)
                    ta.add(s.activation(dst_c, W2B[:, p, :], AF.Sin, bias=CONS[:, 3:4]))

            @blk.tensor
            def _(t):
                mm = Ch(mm_s)
                t.wait_ge(agl, 96)

                def q_tile(g_i):
                    p = g_i % 2
                    if g_i >= 2:
                        t.wait_ge(tv_s, 8 + 8 * (g_i - 2) + 1)
                    inst = None
                    for c in range(8):
                        inst = t.matmul(TP[p][:], WQT[:, c, :],
                                        XNTS[:, c, g_i * 512:(g_i + 1) * 512],
                                        start=(c == 0), stop=(c == 7))
                    mm.add(inst)

                def k_tile(g_i, off):
                    p = g_i % 2
                    t.wait_ge(tv_s, 8 + 8 * (g_i - 2) + 1)
                    inst = None
                    for c in range(8):
                        inst = t.matmul(TP[p][:], WKT[:, c, :],
                                        XNTNH[:, c, (g_i - 4) * 512 - off:(g_i - 3) * 512 - off],
                                        start=(c == 0), stop=(c == 7))
                    mm.add(inst)

                def v_tile(vt, off):
                    p = vt % 2
                    if vt >= 2:
                        t.wait_ge(ve_s, vt - 1)
                    for c in range(8):
                        t.matmul(VP[p][:], XNTNH[:, c, vt * 128 - off:(vt + 1) * 128 - off],
                                 WVT[:, c, :], start=(c == 0), stop=False)
                    mm.add(t.matmul(VP[p][:], ONESB[:], BVROW[:], start=False, stop=True))

                for g_i in range(4):
                    q_tile(g_i)
                t.wait_ge(adl, 80)
                for g_i in range(4, 8):
                    k_tile(g_i, 0)
                for vt in range(16):
                    v_tile(vt, 0)
                t.wait_ge(adl, 96)
                for g_i in range(8, 12):
                    k_tile(g_i, 2048)
                for vt in range(16, 32):
                    v_tile(vt, 2048)

        a1es.close()

        bes = ExitStack()
        bsb = lambda n, shp, dt=F32: bes.enter_context(nc.sbuf_tensor(n, list(shp), dt))
        bps = lambda n, shp, dt=F32: bes.enter_context(nc.psum_tensor(n, list(shp), dt))
        with bes:
            PBUF = bsb("PBUF", (128, 32, 512), BF16)
            OSB = bsb("OSB", (128, 2, HD), BF16)
            OTS = bsb("OTS", (128, HD), BF16)
            RLL = bsb("RLL", (128, 2))
            SPA = bps("SPA", (128, 512)); SPB = bps("SPB", (128, 512))
            PVPA = bps("PVPA", (128, 130)); PVPB = bps("PVPB", (128, 130))
            OTP = bps("OTP", (128, HD), BF16)
            YPA = bps("YPA", (128, 512)); YPB = bps("YPB", (128, 512))
            blk = bes.enter_context(nc.Block())
            SP = [SPA, SPB]
            PVP = [PVPA, PVPB]
            YP = [YPA, YPB]

            @blk.tensor
            def _(t):
                sp = Ch(sp_s)
                pv = Ch(pv_s)
                ym = Ch(ym_s)
                for qt in range(4):
                    for kc in range(32):
                        g = qt * 32 + kc
                        if g >= 2:
                            t.wait_ge(px_s, g - 1)
                        t.matmul(SP[g % 2][:], CKT[:, kc * 128:(kc + 1) * 128],
                                 CQT[:, qt * 512:(qt + 1) * 512], start=True, stop=False)
                        sp.add(t.matmul(SP[g % 2][:], SKT[:, kc * 128:(kc + 1) * 128],
                                        SQT[:, qt * 512:(qt + 1) * 512],
                                        start=False, stop=True))
                    for qsub in range(4):
                        gq = qt * 4 + qsub
                        t.wait_ge(px_s, 32 * (qt + 1))
                        if gq >= 2:
                            t.wait_ge(ep_s, 2 * gq - 2)
                        inst = None
                        for kc in range(32):
                            inst = t.matmul(PVP[gq % 2][:, 0:129],
                                            PBUF[:, kc, qsub * 128:(qsub + 1) * 128],
                                            VV[:, kc, 0:129],
                                            start=(kc == 0), stop=(kc == 31))
                        pv.add(inst)
                        t.wait_ge(ep_s, 2 * gq + 2)
                        if gq >= 1:
                            t.wait_ge(ot_s, 3 * gq)
                        t.transpose(OTP[:], OSB[:, gq % 2, :], IDENTB[:]).then_inc(ot_s, 1)
                        t.wait_ge(ot_s, 3 * gq + 3)
                        if gq >= 1:
                            t.wait_ge(xa_s, 2 * gq)
                        for dh in range(2):
                            t.matmul(YP[dh][:], OTS[:], WOTS[:, dh * 512:(dh + 1) * 512],
                                     start=True, stop=False)
                            ym.add(t.matmul(YP[dh][:], ONESB[:],
                                            BOROW[:, dh * 512:(dh + 1) * 512],
                                            start=False, stop=True))

            @blk.scalar
            def _(s):
                px = Ch(px_s)
                for qt in range(4):
                    for kc in range(32):
                        g = qt * 32 + kc
                        s.wait_ge(sp_s, g + 1)
                        if qt >= 1 and kc == 0:
                            s.wait_ge(pv_s, 4 * qt)
                        px.add(s.activation(PBUF[:, kc, :], SP[g % 2][:],
                                            AF.Exp, scale=CONS[:, 2:3]))

            @blk.vector
            def _(v):
                ep = Ch(ep_s)
                xa = Ch(xa_s)
                ot = 0
                for qt in range(4):
                    for qsub in range(4):
                        gq = qt * 4 + qsub
                        v.wait_ge(pv_s, gq + 1)
                        ep.add(v.reciprocal(RLL[:, gq % 2:gq % 2 + 1],
                                            PVP[gq % 2][:, 128:129]))
                        ep.w(v)
                        ep.add(v.tensor_scalar(OSB[:, gq % 2, :], PVP[gq % 2][:, 0:128],
                                               RLL[:, gq % 2:gq % 2 + 1], None, ALU.mult))
                        v.wait_ge(ot_s, 3 * gq + 1)
                        if gq >= 1:
                            v.wait_ge(ym_s, 2 * gq)
                        v.tensor_copy(OTS[:], OTP[:]).then_inc(ot_s, 2)
                        for dh in range(2):
                            v.wait_ge(ym_s, 2 * gq + dh + 1)
                            xa.add(v.tensor_tensor(X16[:, gq, dh * 512:(dh + 1) * 512],
                                                   YP[dh][:],
                                                   X16[:, gq, dh * 512:(dh + 1) * 512],
                                                   ALU.add))



def _emit_shared(nc, d):
    es = d["es"]
    gw_s = es.enter_context(nc.semaphore("sgw"))
    g1p = [es.enter_context(nc.semaphore("sg1a")), es.enter_context(nc.semaphore("sg1b"))]
    g2p = [es.enter_context(nc.semaphore("sg2a")), es.enter_context(nc.semaphore("sg2b"))]
    hm_s = es.enter_context(nc.semaphore("shm"))
    hx_s = es.enter_context(nc.semaphore("shx"))
    sm_s = es.enter_context(nc.semaphore("ssm"))
    sv_s = es.enter_context(nc.semaphore("ssv"))
    XNTS, X16, ONESB = d["XNTS"], d["X16"], d["ONESB"]

    ses = ExitStack()
    ssb = lambda n, shp, dt=F32: ses.enter_context(nc.sbuf_tensor(n, list(shp), dt))
    sps = lambda n, shp, dt=F32: ses.enter_context(nc.psum_tensor(n, list(shp), dt))
    with ses:
        HT = ssb("HT", (128, 32, 512), BF16)
        SH1C = ssb("SH1C", (128, 2, 8, 128), BF16)
        SH2H = ssb("SH2H", (128, 2, 512), BF16)
        SH1BS = ssb("SH1BS", (128, 32))
        SH2BROW = ssb("SH2BROW", (1, D), BF16)
        HP0 = sps("HP0", (128, 512)); HP1 = sps("HP1", (128, 512))
        SYP = sps("SYP", (128, 4, 512))
        HP = [HP0, HP1]
        blk = ses.enter_context(nc.Block())

        def smv(q):
            return 33 * (q // 32) + q % 32 + 1

        @blk.gpsimd
        def _(g):
            gw = Ch(gw_s)
            gw.add(g.dma_start(SH1BS[:], d["sh1b"][:, :]), 16)
            gw.add(g.dma_start(SH2BROW[:], d["sh2b"][0:1, :]), 16)
            for tq in range(4):
                for shc in range(32):
                    g1 = tq * 32 + shc
                    if g1 >= 2:
                        g.wait_ge(hm_s, g1 - 1)
                    g.dma_start(SH1C[:, g1 % 2, :, :],
                                d["sh1t"][:, shc * 128:(shc + 1) * 128]
                                .rearrange("(c p) h -> p c h", p=128)).then_inc(g1p[g1 % 2], 16)
                for dh in range(2):
                    for shc in range(32):
                        q = (tq * 2 + dh) * 32 + shc
                        if q >= 2:
                            g.wait_ge(sm_s, smv(q - 2))
                        g.dma_start(SH2H[:, q % 2, :],
                                    d["sh2t"][shc * 128:(shc + 1) * 128,
                                              dh * 512:(dh + 1) * 512]
                                    .rearrange("(c p) dd -> p c dd", p=128)).then_inc(g2p[q % 2], 16)

        @blk.tensor
        def _(t):
            hm = Ch(hm_s)
            sm = Ch(sm_s)
            for tq in range(4):
                for shc in range(32):
                    g1 = tq * 32 + shc
                    t.wait_ge(g1p[g1 % 2], 16 * (g1 // 2 + 1))
                    if g1 >= 2:
                        t.wait_ge(hx_s, g1 - 1)
                    inst = None
                    for c in range(8):
                        inst = t.matmul(HP[g1 % 2][:], SH1C[:, g1 % 2, c, :],
                                        XNTS[:, c, tq * 512:(tq + 1) * 512],
                                        start=(c == 0), stop=(c == 7))
                    hm.add(inst)
                for dh in range(2):
                    t.wait_ge(hx_s, 32 * (tq + 1))
                    if tq * 2 + dh >= 1:
                        t.wait_ge(sv_s, 4 * (tq * 2 + dh))
                    for shc in range(32):
                        g2 = (tq * 2 + dh) * 32 + shc
                        t.wait_ge(g2p[g2 % 2], 16 * (g2 // 2 + 1))
                        for tsub in range(4):
                            inst = t.matmul(SYP[:, tsub, :],
                                            HT[:, shc, tsub * 128:(tsub + 1) * 128],
                                            SH2H[:, (g2) % 2, :],
                                            start=(shc == 0), stop=False)
                        sm.add(inst)
                    for tsub in range(4):
                        inst = t.matmul(SYP[:, tsub, :], ONESB[:],
                                        SH2BROW[:, dh * 512:(dh + 1) * 512],
                                        start=False, stop=True)
                    sm.add(inst)

        @blk.scalar
        def _(s):
            hx = Ch(hx_s)
            s.wait_ge(gw_s, 32)
            for tq in range(4):
                if tq >= 1:
                    s.wait_ge(sm_s, 33 * 2 * tq)
                for shc in range(32):
                    g1 = tq * 32 + shc
                    s.wait_ge(hm_s, g1 + 1)
                    hx.add(s.activation(HT[:, shc, :], HP[g1 % 2][:], AF.Gelu,
                                        bias=SH1BS[:, shc:shc + 1]))

        @blk.vector
        def _(v):
            sv = Ch(sv_s)
            for tq in range(4):
                for dh in range(2):
                    v.wait_ge(sm_s, 33 * (tq * 2 + dh + 1))
                    for tsub in range(4):
                        sv.add(v.tensor_tensor(
                            X16[:, tq * 4 + tsub, dh * 512:(dh + 1) * 512],
                            SYP[:, tsub, :],
                            X16[:, tq * 4 + tsub, dh * 512:(dh + 1) * 512],
                            ALU.add))


def _emit_moe(nc, d):
    es = d["es"]
    mw_s = es.enter_context(nc.semaphore("mmw"))
    mwp = [es.enter_context(nc.semaphore("mw1a")), es.enter_context(nc.semaphore("mw1b")),
           es.enter_context(nc.semaphore("mw2a")), es.enter_context(nc.semaphore("mw2b"))]
    me_s = es.enter_context(nc.semaphore("mme"))
    mg_s = es.enter_context(nc.semaphore("mmg"))
    mv_s = es.enter_context(nc.semaphore("mmv"))
    XNTS, X16, GTB, SELC = d["XNTS"], d["X16"], d["GTB"], d["SELC"]

    mes = ExitStack()
    msb = lambda n, shp, dt=F32: mes.enter_context(nc.sbuf_tensor(n, list(shp), dt))
    mps = lambda n, shp, dt=F32: mes.enter_context(nc.psum_tensor(n, list(shp), dt))
    with mes:
        HGT = msb("HGT", (128, 16, 512), BF16)
        HGA = msb("HGA", (128, 2, 512), BF16)
        GEB = msb("GEB", (128, 512), BF16)
        FC1H = msb("FC1H", (128, 2, 8, 1024), BF16)
        FC2H = msb("FC2H", (128, 2, 8, 1024), BF16)
        FB1 = msb("FB1", (128, 8, 16))
        FC2BAS = msb("FC2BAS", (8, D), BF16)
        PSALL = mps("PSALL", (128, 6, 512))
        blk = mes.enter_context(nc.Block())

        NI = 32

        @blk.gpsimd
        def _(g):
            mw = Ch(mw_s)
            mw.add(g.dma_start(FB1[:], d["fc1b"].rearrange("e p c -> p e c")), 16)
            mw.add(g.dma_start(FC2BAS[:], d["fc2ba"][:, :]), 16)
            for i in range(NI):
                tq, e = divmod(i, 8)
                if i >= 1:
                    g.wait_ge(me_s, 51 * (i - 1) + 9)
                g.dma_start(FC1H[:, 0, :, :],
                            d["fc1t"][e, :, 0:1024]
                            .rearrange("(c p) h -> p c h", p=128)).then_inc(mwp[0], 16)
                if i >= 1:
                    g.wait_ge(me_s, 51 * (i - 1) + 17)
                g.dma_start(FC1H[:, 1, :, :],
                            d["fc1t"][e, :, 1024:2048]
                            .rearrange("(c p) h -> p c h", p=128)).then_inc(mwp[1], 16)
                if i >= 1:
                    g.wait_ge(me_s, 51 * (i - 1) + 42)
                g.dma_start(FC2H[:, 0, :, :],
                            d["fc2t"][e, 0:1024, :]
                            .rearrange("(c p) dd -> p c dd", p=128)).then_inc(mwp[2], 16)
                if i >= 1:
                    g.wait_ge(me_s, 51 * (i - 1) + 50)
                g.dma_start(FC2H[:, 1, :, :],
                            d["fc2t"][e, 1024:2048, :]
                            .rearrange("(c p) dd -> p c dd", p=128)).then_inc(mwp[3], 16)

        @blk.tensor
        def _(t):
            me = Ch(me_s)
            t.wait_ge(mw_s, 32)
            for i in range(NI):
                tq, e = divmod(i, 8)
                if i >= 1:
                    t.wait_ge(mg_s, 16 * i)
                me.add(t.matmul(PSALL[:, 0, :], SELC[:, e, :],
                                GTB[:, tq * 512:(tq + 1) * 512], start=True, stop=True))
                for ehc in range(16):
                    half, cc = divmod(ehc, 8)
                    t.wait_ge(mwp[half], 16 * (i + 1))
                    if ehc == 0:
                        t.wait_ge(mv_s, 25 * i + 1)
                    elif ehc >= 2:
                        t.wait_ge(mg_s, 16 * i + ehc - 1)
                    inst = None
                    for c in range(8):
                        inst = t.matmul(PSALL[:, ehc % 2, :],
                                        FC1H[:, half, c, cc * 128:(cc + 1) * 128],
                                        XNTS[:, c, tq * 512:(tq + 1) * 512],
                                        start=(c == 0), stop=(c == 7))
                    me.add(inst)
                for dh in range(2):
                    for ehc in range(16):
                        half, cc = divmod(ehc, 8)
                        t.wait_ge(mwp[2 + half], 16 * (i + 1))
                        if dh == 0 and ehc == 0:
                            t.wait_ge(mv_s, 25 * i + 17)
                            if i >= 1:
                                t.wait_ge(mv_s, 25 * (i - 1) + 25)
                        if dh == 1 and ehc == 0:
                            t.wait_ge(mv_s, 25 * i + 21)
                        for tsub in range(4):
                            inst = t.matmul(PSALL[:, 2 + tsub, :],
                                            HGT[:, ehc, tsub * 128:(tsub + 1) * 128],
                                            FC2H[:, half, cc, dh * 512:(dh + 1) * 512],
                                            start=(ehc == 0), stop=False)
                        me.add(inst)
                    for tsub in range(4):
                        inst = t.matmul(PSALL[:, 2 + tsub, :],
                                        GTB[:, (tq * 512 + tsub * 128):(tq * 512 + tsub * 128 + 128)],
                                        FC2BAS[:, dh * 512:(dh + 1) * 512],
                                        start=False, stop=True)
                    me.add(inst)

        @blk.scalar
        def _(s):
            mg = Ch(mg_s)
            s.wait_ge(mw_s, 32)
            for i in range(NI):
                tq, e = divmod(i, 8)
                for ehc in range(16):
                    s.wait_ge(me_s, 51 * i + 1 + ehc + 1)
                    if ehc >= 2:
                        s.wait_ge(mv_s, 25 * i + 1 + ehc - 1)
                    mg.add(s.activation(HGA[:, ehc % 2, :], PSALL[:, ehc % 2, :],
                                        AF.Gelu, bias=FB1[:, e, ehc:ehc + 1]))

        @blk.vector
        def _(v):
            mv = Ch(mv_s)
            for i in range(NI):
                tq, e = divmod(i, 8)
                v.wait_ge(me_s, 51 * i + 1)
                mv.add(v.tensor_copy(GEB[:], PSALL[:, 0, :]))
                for ehc in range(16):
                    v.wait_ge(mg_s, 16 * i + ehc + 1)
                    if ehc == 0 and i >= 1:
                        v.wait_ge(me_s, 51 * i)
                    mv.add(v.tensor_tensor(HGT[:, ehc, :], HGA[:, ehc % 2, :],
                                           GEB[:], ALU.mult))
                for dh in range(2):
                    v.wait_ge(me_s, 51 * i + 18 + 17 * dh + 16)
                    for tsub in range(4):
                        mv.add(v.tensor_tensor(
                            X16[:, tq * 4 + tsub, dh * 512:(dh + 1) * 512],
                            PSALL[:, 2 + tsub, :],
                            X16[:, tq * 4 + tsub, dh * 512:(dh + 1) * 512],
                            ALU.add))



def _emit_ln1_stores(nc, d):
    es = d["es"]
    la_s = es.enter_context(nc.semaphore("la"))
    lv_s = es.enter_context(nc.semaphore("lv"))
    lt_s = es.enter_context(nc.semaphore("lt"))
    le_s = es.enter_context(nc.semaphore("le"))
    od_s = es.enter_context(nc.semaphore("odb"))
    STAT = d["STAT"]
    with nc.sbuf_tensor("SQ1", [128, 2, D], F32) as SQ1, \
         nc.sbuf_tensor("XN1", [128, 2, D], F32) as XN1, \
         nc.psum_tensor("PTA", [128, 128], F32) as PTA, \
         nc.psum_tensor("PTB", [128, 128], F32) as PTB, \
         nc.Block() as blk:
        PT2 = [PTA, PTB]

        @blk.scalar
        def _(s):
            la = Ch(la_s)
            for n in range(NTILE):
                if n >= 2:
                    s.wait_ge(la_s, n - 1)
                la.add(s.activation(SQ1[:, n % 2, :], d["X16"][:, n, :], AF.Square,
                                    accum_out=STAT[:, 1, n:n + 1]))
            s.wait_ge(lv_s, NTILE + 4)
            la.add(s.activation(STAT[:, 5, :], STAT[:, 4, :], AF.Ln, bias=d["CONS"][:, 0:1]))
            s.wait_ge(la_s, NTILE + 1)
            la.add(s.activation(STAT[:, 5, :], STAT[:, 5, :], AF.Exp, scale=d["CONS"][:, 1:2]))

        @blk.vector
        def _(v):
            lv = Ch(lv_s)
            for n in range(NTILE):
                lv.add(v.reduce_sum(STAT[:, 0, n:n + 1], d["X16"][:, n, :], AX.X))
            lv.w(v)
            v.wait_ge(la_s, NTILE)
            lv.add(v.tensor_scalar(STAT[:, 2, :], STAT[:, 0, :], 1.0 / D, None, ALU.mult))
            lv.w(v)
            lv.add(v.tensor_scalar(STAT[:, 3, :], STAT[:, 1, :], 1.0 / D, None, ALU.mult))
            lv.w(v)
            lv.add(v.tensor_tensor(STAT[:, 4, :], STAT[:, 2, :], STAT[:, 2, :], ALU.mult))
            lv.w(v)
            lv.add(v.tensor_tensor(STAT[:, 4, :], STAT[:, 3, :], STAT[:, 4, :], ALU.subtract))
            v.wait_ge(la_s, NTILE + 2)
            le = Ch(le_s)
            def _evacs(nn):
                for j in range(8 * nn, 8 * nn + 8):
                    v.wait_ge(lt_s, j + 1)
                    _, k = divmod(j, 8)
                    le.add(v.tensor_copy(d["XNTS"][:, k, nn * 128:(nn + 1) * 128],
                                         PT2[j % 2][:]))
            for n in range(NTILE):
                if n >= 2:
                    v.wait_ge(lt_s, 8 * (n - 1))
                lv.add(v.tensor_scalar(XN1[:, n % 2, :], d["X16"][:, n, :],
                                       STAT[:, 2, n:n + 1], STAT[:, 5, n:n + 1],
                                       ALU.subtract, ALU.mult))
                if n >= 1:
                    _evacs(n - 1)
            _evacs(NTILE - 1)

        @blk.tensor
        def _(t):
            lt = Ch(lt_s)
            for j in range(NTILE * 8):
                n, k = divmod(j, 8)
                t.wait_ge(lv_s, NTILE + 4 + n + 1)
                if j >= 2:
                    t.wait_ge(le_s, j - 1)
                lt.add(t.transpose(PT2[j % 2][:], XN1[:, n % 2, k * 128:(k + 1) * 128],
                                   d["IDENT"][:]))

        @blk.sync
        def _(s):
            od = Ch(od_s)
            od.add(s.dma_start(d["x1"].rearrange("(n p) dd -> p n dd", p=128), d["X16"][:]), 16)
            s.wait_ge(le_s, NTILE * 8)
            od.add(s.dma_start(d["xnt1"].rearrange("(c p) t -> p c t", p=128), d["XNTS"][:]), 16)
            s.wait_ge(od_s, 32)



KCNT = [13, 17, 29, 32]
NKC = sum(KCNT)


def build_launch_c():
    nc = bass.Bass("TRN2", target_bir_lowering=False)
    d = {}
    d["xc"] = nc.dram_tensor("xc", [NT, D], F32, kind="ExternalInput").ap()
    d["xnts"] = nc.dram_tensor("xnts", [D, NT], BF16, kind="ExternalInput").ap()
    d["xntn"] = nc.dram_tensor("xntn", [D, T], BF16, kind="ExternalInput").ap()
    d["qpos"] = nc.dram_tensor("qpos", [1, NT], F32, kind="ExternalInput").ap()
    for nm, shp in [("wqtp", [D, D]), ("wktp", [D, D]), ("wvt1", [D, D]),
                    ("costq", [512, NT]), ("sintq", [512, NT]),
                    ("costk", [512, T]), ("sintk", [512, T]),
                    ("selt", [D, E1]), ("keyst", [D, E1 * J1]),
                    ("valsc", [E1 * J1, D]), ("rsc", [1, 1])]:
        d[nm] = nc.dram_tensor(nm, shp, F32, kind="ExternalInput").ap()
    d["xo"] = nc.dram_tensor("xo", [NT, D], F32, kind="ExternalOutput").ap()
    d["krt_d"] = nc.dram_tensor("krt_d", [D, T], BF16, kind="Internal").ap()
    d["qrt_d"] = nc.dram_tensor("qrt_d", [D, NT], BF16, kind="Internal").ap()
    d["v1_d"] = nc.dram_tensor("v1_d", [32, 128, 2, 513], BF16, kind="Internal").ap()

    es = ExitStack()
    d["es"] = es
    sb = lambda n, s, dt=F32: es.enter_context(nc.sbuf_tensor(n, list(s), dt))
    with es:
        d["X16"] = sb("X16", (128, NTILE, D))
        d["XNTS"] = sb("XNTS", (128, 8, NT), BF16)
        d["QPOSB"] = sb("QPOSB", (128, NT))
        d["PIOTA"] = sb("PIOTA", (128, 1))
        d["ONESB"] = sb("ONESB", (1, 128), BF16)
        d["ONESF"] = sb("ONESF", (1, 128))
        d["IDENT"] = sb("IDENT", (128, 128))
        d["CONS"] = sb("CONS", (128, 4))
        d["ONESC"] = sb("ONESC", (128, 1), BF16)
        d["dl_s"] = es.enter_context(nc.semaphore("cdl"))

        _c0_loads(nc, d)
        _emit_qkv1(nc, d)
        _emit_attn1(nc, d)
        _emit_ffn1(nc, d)
        _emit_store_c(nc, d)
    return nc


def _c0_loads(nc, d):
    dl = Ch(d["dl_s"])
    cg1 = d["es"].enter_context(nc.semaphore("ccg1"))
    with nc.Block() as blk:
        @blk.sync
        def _(s):
            dl.add(s.dma_start(d["X16"][:], d["xc"].rearrange("(n p) dd -> p n dd", p=128)), 16)
            dl.add(s.dma_start(d["XNTS"][:], d["xnts"].rearrange("(c p) t -> p c t", p=128)), 16)

        @blk.gpsimd
        def _(g):
            g.iota(d["PIOTA"][:], [[0, 1]], channel_multiplier=1,
                   allow_small_or_imprecise_dtypes=True)
            g.memset(d["IDENT"][:], 0.0).then_inc(cg1, 1)
            g.wait_ge(cg1, 1)
            g.affine_select(out=d["IDENT"][:], in_=d["IDENT"][:], compare_op=ALU.not_equal,
                            fill=1.0, base=0, pattern=[[-1, 128]], channel_multiplier=1)

        @blk.vector
        def _(v):
            v.memset(d["ONESB"][:], 1.0)
            v.memset(d["ONESF"][:], 1.0)
            v.memset(d["ONESC"][:], 1.0)
            v.memset(d["CONS"][:, 0:1], 1.0 / 32.0)


def _emit_qkv1(nc, d):
    es = d["es"]
    cdl2 = es.enter_context(nc.semaphore("cdl2"))
    cwp = [es.enter_context(nc.semaphore("cwpa")), es.enter_context(nc.semaphore("cwpb"))]
    cwv = es.enter_context(nc.semaphore("cwv"))
    ctb = [es.enter_context(nc.semaphore("ctba")), es.enter_context(nc.semaphore("ctbb"))]
    cmm_s = es.enter_context(nc.semaphore("cmm"))
    cev_s = es.enter_context(nc.semaphore("cev"))
    crt_s = es.enter_context(nc.semaphore("crt"))
    cst_s = es.enter_context(nc.semaphore("cst"))
    cd3 = es.enter_context(nc.semaphore("cd3"))
    cm2_s = es.enter_context(nc.semaphore("cm2"))
    ce2_s = es.enter_context(nc.semaphore("ce2"))
    cs2p = [es.enter_context(nc.semaphore("cs2a")), es.enter_context(nc.semaphore("cs2b"))]

    qes = ExitStack()
    qsb = lambda n, shp, dt=F32: qes.enter_context(nc.sbuf_tensor(n, list(shp), dt))
    qps = lambda n, shp, dt=F32: qes.enter_context(nc.psum_tensor(n, list(shp), dt))
    with qes:
        XNTNH = qsb("XNTNH", (128, 8, 2048), BF16)

        q2 = ExitStack()
        q2sb = lambda n, shp, dt=F32: q2.enter_context(nc.sbuf_tensor(n, list(shp), dt))
        q2ps = lambda n, shp, dt=F32: q2.enter_context(nc.psum_tensor(n, list(shp), dt))
        q2.__enter__()
        WCHK = q2sb("WCHK", (128, 2, 8, 128), BF16)
        TBC = q2sb("TBC", (128, 2, 4, 512), BF16)
        TBS = q2sb("TBS", (128, 2, 4, 512), BF16)
        QPT = q2sb("QPT", (128, 2, 8, 512), BF16)
        STAGE = q2sb("STAGE", (128, 8, 512), BF16)
        RT1 = q2sb("RT1", (128, 512), BF16)
        RT2 = q2sb("RT2", (128, 512), BF16)
        QPP = [q2ps("QPP0", (128, 512)), q2ps("QPP1", (128, 512))]
        QBP = q2ps("QBP", (128, 512))

        with nc.Block() as blk:
            @blk.sync
            def _(s):
                c = Ch(cdl2)
                c.add(s.dma_start(XNTNH[:], d["xntn"][:, 0:2048]
                                  .rearrange("(c p) t -> p c t", p=128)), 16)
                c.add(s.dma_start(d["QPOSB"][0:1, :], d["qpos"][:, :]), 16)
                st = Ch(cst_s)

                def store(g):
                    s.wait_ge(crt_s, 24 * (g + 1))
                    dst = d["qrt_d"][:, g * 512:(g + 1) * 512] if g < 4 else \
                        d["krt_d"][:, (g - 4) * 512:(g - 3) * 512]
                    st.add(s.dma_start(dst.rearrange("(c p) t -> p c t", p=128),
                                       STAGE[:]), 16)

                for g in range(8):
                    store(g)
                c.add(s.dma_start(XNTNH[:], d["xntn"][:, 2048:4096]
                                  .rearrange("(c p) t -> p c t", p=128)), 16)
                for g in range(8, 12):
                    store(g)
                s.wait_ge(cst_s, 16 * 12)

            @blk.gpsimd
            def _(g):
                for g_i in range(12):
                    wsrc = d["wqtp"] if g_i < 4 else d["wktp"]
                    for f in range(8):
                        gp = g_i * 8 + f
                        if gp >= 2:
                            g.wait_ge(cmm_s, 4 + gp - 1)
                        g.dma_start(WCHK[:, gp % 2, :, :],
                                    wsrc[:, f * 128:(f + 1) * 128]
                                    .rearrange("(c p) h -> p c h", p=128)
                                    ).then_inc(cwp[gp % 2], 16)
                    if g_i >= 2:
                        g.wait_ge(crt_s, 24 * (g_i - 1))
                    if g_i < 4:
                        cs_, sn_, off = d["costq"], d["sintq"], g_i * 512
                    else:
                        cs_, sn_, off = d["costk"], d["sintk"], (g_i - 4) * 512
                    g.dma_start(TBC[:, g_i % 2, :, :],
                                cs_[:, off:off + 512]
                                .rearrange("(c p) t -> p c t", p=128)).then_inc(ctb[g_i % 2], 16)
                    g.dma_start(TBS[:, g_i % 2, :, :],
                                sn_[:, off:off + 512]
                                .rearrange("(c p) t -> p c t", p=128)).then_inc(ctb[g_i % 2], 16)

            @blk.tensor
            def _(t):
                mm = Ch(cmm_s)
                t.wait_ge(cdl2, 32)
                t.wait_ge(d["dl_s"], 32)
                for i4 in range(4):
                    if i4 >= 1:
                        t.wait_ge(cev_s, i4)
                    mm.add(t.matmul(QBP[:], d["ONESF"][:],
                                    d["QPOSB"][0:1, i4 * 512:(i4 + 1) * 512],
                                    start=True, stop=True))
                for g_i in range(12):
                    src_off = g_i * 512 if g_i < 4 else (g_i - 4) * 512 % 2048
                    src = d["XNTS"] if g_i < 4 else XNTNH
                    if g_i >= 8:
                        if g_i == 8:
                            t.wait_ge(cdl2, 48)
                    for f in range(8):
                        gp = g_i * 8 + f
                        t.wait_ge(cwp[gp % 2], 16 * (gp // 2 + 1))
                        if gp >= 2:
                            t.wait_ge(cev_s, 4 + gp - 1)
                        inst = None
                        for c in range(8):
                            inst = t.matmul(QPP[gp % 2][:], WCHK[:, gp % 2, c, :],
                                            src[:, c, src_off:src_off + 512],
                                            start=(c == 0), stop=(c == 7))
                        mm.add(inst)

            @blk.vector
            def _(v):
                ev = Ch(cev_s)
                rt = Ch(crt_s)
                for i4 in range(4):
                    v.wait_ge(cmm_s, i4 + 1)
                    ev.add(v.tensor_copy(d["QPOSB"][:, i4 * 512:(i4 + 1) * 512], QBP[:]))
                for g_i in range(12):
                    if g_i >= 2:
                        v.wait_ge(crt_s, 24 * (g_i - 1))
                    for f in range(8):
                        gp = g_i * 8 + f
                        v.wait_ge(cmm_s, 4 + gp + 1)
                        ev.add(v.tensor_copy(QPT[:, g_i % 2, f, :], QPP[gp % 2][:]))
                    p = g_i % 2
                    v.wait_ge(ctb[p], 32 * (g_i // 2 + 1))
                    v.wait_ge(cst_s, 16 * g_i)
                    for f in range(4):
                        lo = QPT[:, p, f, :]
                        hi = QPT[:, p, f + 4, :]
                        c_ = TBC[:, p, f, :]
                        s_ = TBS[:, p, f, :]
                        rt.add(v.tensor_tensor(RT1[:], lo, c_, ALU.mult)); rt.w(v)
                        rt.add(v.tensor_tensor(RT2[:], hi, s_, ALU.mult)); rt.w(v)
                        rt.add(v.tensor_tensor(STAGE[:, f, :], RT1[:], RT2[:],
                                               ALU.subtract)); rt.w(v)
                        rt.add(v.tensor_tensor(RT1[:], lo, s_, ALU.mult)); rt.w(v)
                        rt.add(v.tensor_tensor(RT2[:], hi, c_, ALU.mult)); rt.w(v)
                        rt.add(v.tensor_tensor(STAGE[:, f + 4, :], RT1[:], RT2[:],
                                               ALU.add))
                        if f < 3:
                            rt.w(v)
        q2.close()

        v2 = ExitStack()
        v2sb = lambda n, shp, dt=F32: v2.enter_context(nc.sbuf_tensor(n, list(shp), dt))
        v2ps = lambda n, shp, dt=F32: v2.enter_context(nc.psum_tensor(n, list(shp), dt))
        v2.__enter__()
        WVP = v2sb("WVP", (128, 8, D), BF16)
        VT1 = v2sb("VT1", (128, 2, 2, 513), BF16)
        VPP = [v2ps("VPP0", (128, 512)), v2ps("VPP1", (128, 512))]

        with nc.Block() as blk:
            @blk.sync
            def _(s):
                c = Ch(cd3)
                c.add(s.dma_start(XNTNH[:], d["xntn"][:, 0:2048]
                                  .rearrange("(c p) t -> p c t", p=128)), 16)
                def storev(vt):
                    s.wait_ge(ce2_s, 4 * (vt + 1))
                    s.dma_start(d["v1_d"][vt, :, :, :],
                                VT1[:, vt % 2, :, :]).then_inc(cs2p[vt % 2], 16)

                for vt in range(16):
                    storev(vt)
                c.add(s.dma_start(XNTNH[:], d["xntn"][:, 2048:4096]
                                  .rearrange("(c p) t -> p c t", p=128)), 16)
                for vt in range(16, 32):
                    storev(vt)
                s.wait_ge(cs2p[0], 16 * 16)
                s.wait_ge(cs2p[1], 16 * 16)

            @blk.gpsimd
            def _(g):
                g.dma_start(WVP[:], d["wvt1"].rearrange("(c p) f -> p c f", p=128)
                            ).then_inc(cwv, 16)

            @blk.tensor
            def _(t):
                m2 = Ch(cm2_s)
                t.wait_ge(cwv, 16)
                t.wait_ge(cd3, 16)
                for vt in range(32):
                    off = 0 if vt < 16 else 2048
                    if vt == 16:
                        t.wait_ge(cd3, 32)
                    for dh in range(2):
                        gv = vt * 2 + dh
                        if gv >= 2:
                            t.wait_ge(ce2_s, 4 * ((gv - 2) // 2) + (gv - 2) % 2 + 1)
                        inst = None
                        for c in range(8):
                            inst = t.matmul(VPP[gv % 2][:],
                                            XNTNH[:, c, vt * 128 - off:(vt + 1) * 128 - off],
                                            WVP[:, c, dh * 512:(dh + 1) * 512],
                                            start=(c == 0), stop=(c == 7))
                        m2.add(inst)

            @blk.vector
            def _(v):
                e2 = Ch(ce2_s)
                for vt in range(32):
                    for dh in range(2):
                        gv = vt * 2 + dh
                        v.wait_ge(cm2_s, gv + 1)
                        if dh == 0 and vt >= 2:
                            v.wait_ge(cs2p[vt % 2], 16 * (vt // 2))
                        e2.add(v.tensor_copy(VT1[:, vt % 2, dh, 0:512], VPP[gv % 2][:]))
                    e2.add(v.memset(VT1[:, vt % 2, 0, 512:513], 1.0))
                    e2.add(v.memset(VT1[:, vt % 2, 1, 512:513], 1.0))
        v2.close()


def _emit_attn1(nc, d):
    es = d["es"]
    kst = [es.enter_context(nc.semaphore("kka")), es.enter_context(nc.semaphore("kkb"))]
    vst = [es.enter_context(nc.semaphore("vva")), es.enter_context(nc.semaphore("vvb"))]
    sp_s = es.enter_context(nc.semaphore("csp"))
    px_s = es.enter_context(nc.semaphore("cpx"))
    pm_s = es.enter_context(nc.semaphore("cpm"))
    pv_s = es.enter_context(nc.semaphore("cpv"))
    xa_s = es.enter_context(nc.semaphore("cxa"))

    QPOSB, X16, PIOTA, CONS = d["QPOSB"], d["X16"], d["PIOTA"], d["CONS"]
    qs2 = [es.enter_context(nc.semaphore("cqca")), es.enter_context(nc.semaphore("cqcb"))]

    gidx = {}
    gi = 0
    for qt in range(4):
        for kc in range(KCNT[qt]):
            gidx[(qt, kc)] = gi
            gi += 1

    aes = ExitStack()
    asb = lambda n, shp, dt=F32: aes.enter_context(nc.sbuf_tensor(n, list(shp), dt))
    aps = lambda n, shp, dt=F32: aes.enter_context(nc.psum_tensor(n, list(shp), dt))
    with aes:
        PBUF = asb("PBUF1", (128, 32, 512), BF16)
        QRC = asb("QRC", (128, 2, 8, 512), BF16)
        KRC = asb("KRC", (128, 2, 8, 128), BF16)
        VC = asb("VC", (128, 2, 513), BF16)
        PTMP = asb("PTMP", (128, 2, 512), BF16)
        M01 = asb("M01", (128, 2, 512), BF16)
        PCH = asb("PCH", (128, 2))
        RL1 = asb("RL1", (128, 4))
        ONESC = d["ONESC"]
        SP = [aps("SP1A", (128, 512)), aps("SP1B", (128, 512))]
        PV1 = [aps("PV1%d" % i, (128, 512)) for i in range(4)]
        with nc.Block() as blk:
            @blk.sync
            def _(s):
                for qt in range(4):
                    if qt >= 2:
                        s.wait_ge(sp_s, gidx[(qt - 1, 0)])
                    s.dma_start(QRC[:, qt % 2, :, :],
                                d["qrt_d"][:, qt * 512:(qt + 1) * 512]
                                .rearrange("(c p) t -> p c t", p=128)
                                ).then_inc(qs2[qt % 2], 16)
                    for kc in range(KCNT[qt]):
                        g = gidx[(qt, kc)]
                        if g >= 2:
                            s.wait_ge(sp_s, g - 1)
                        s.dma_start(KRC[:, g % 2, :, :],
                                    d["krt_d"][:, kc * 128:(kc + 1) * 128]
                                    .rearrange("(c p) k -> p c k", p=128)
                                    ).then_inc(kst[g % 2], 16)

            @blk.gpsimd
            def _(g):
                vseq = [(qt, dh, kc) for qt in range(4) for dh in range(2)
                        for kc in range(KCNT[qt])]

                def pvpos(qt, dh, kc):
                    return sum(3 * KCNT[i] for i in range(qt)) + (dh + 1) * KCNT[qt] + kc + 1

                for vg, (qt, dh, kc) in enumerate(vseq):
                    if vg >= 2:
                        g.wait_ge(pv_s, pvpos(*vseq[vg - 2]))
                    g.dma_start(VC[:, vg % 2, :],
                                d["v1_d"][kc, :, dh, :]).then_inc(vst[vg % 2], 16)
            @blk.tensor
            def _(t):
                sp = Ch(sp_s)
                pv = Ch(pv_s)
                for qt in range(4):
                    KC = KCNT[qt]
                    t.wait_ge(qs2[qt % 2], 16 * (qt // 2 + 1))
                    for kc in range(KC):
                        g = gidx[(qt, kc)]
                        t.wait_ge(kst[g % 2], 16 * (g // 2 + 1))
                        if g >= 2:
                            t.wait_ge(px_s, g - 1)
                        inst = None
                        for c in range(8):
                            inst = t.matmul(SP[g % 2][:], KRC[:, g % 2, c, :],
                                            QRC[:, qt % 2, c, :],
                                            start=(c == 0), stop=(c == 7))
                        sp.add(inst)
                    t.wait_ge(pm_s, 3 * (gidx[(qt, KC - 1)] + 1))
                    t.wait_ge(xa_s, 12 * qt)
                    for kc in range(KC):
                        for qsub in range(4):
                            inst = t.matmul(PV1[qsub][:, 0:1],
                                            PBUF[:, kc, qsub * 128:(qsub + 1) * 128],
                                            ONESC[:],
                                            start=(kc == 0), stop=(kc == KC - 1))
                        pv.add(inst)
                    for dh in range(2):
                        vgbase = sum(2 * KCNT[i] for i in range(qt)) + dh * KC
                        t.wait_ge(xa_s, 12 * qt + 4 + 4 * dh)
                        for kc in range(KC):
                            vg = vgbase + kc
                            t.wait_ge(vst[vg % 2], 16 * (vg // 2 + 1))
                            for qsub in range(4):
                                inst = t.matmul(PV1[qsub][:],
                                                PBUF[:, kc, qsub * 128:(qsub + 1) * 128],
                                                VC[:, vg % 2, 0:512],
                                                start=(kc == 0), stop=(kc == KC - 1))
                            pv.add(inst)

            @blk.scalar
            def _(s):
                px = Ch(px_s)
                for qt in range(4):
                    for kc in range(KCNT[qt]):
                        g = gidx[(qt, kc)]
                        s.wait_ge(sp_s, g + 1)
                        if g >= 2:
                            s.wait_ge(pm_s, 3 * (g - 1))
                        px.add(s.activation(PTMP[:, g % 2, :], SP[g % 2][:], AF.Exp,
                                            scale=CONS[:, 0:1]))

            @blk.vector
            def _(v):
                pm = Ch(pm_s)
                xa = Ch(xa_s)
                v.wait_ge(d["dl_s"], 32)
                for qt in range(4):
                    KC = KCNT[qt]
                    for kc in range(KC):
                        g = gidx[(qt, kc)]
                        pm.add(v.tensor_scalar(PCH[:, g % 2:g % 2 + 1], PIOTA[:],
                                               float(128 * kc - 1), None, ALU.add))
                        pm.w(v)
                        pm.add(v.tensor_scalar(M01[:, g % 2, :],
                                               QPOSB[:, qt * 512:(qt + 1) * 512],
                                               PCH[:, g % 2:g % 2 + 1], None, ALU.is_ge))
                        pm.w(v)
                        v.wait_ge(px_s, g + 1)
                        if qt >= 1 and kc == 0:
                            v.wait_ge(pv_s, sum(3 * KCNT[i] for i in range(qt)))
                        pm.add(v.tensor_tensor(PBUF[:, kc, :], PTMP[:, g % 2, :],
                                               M01[:, g % 2, :], ALU.mult))
                    base3 = sum(3 * KCNT[i] for i in range(qt))
                    for qsub in range(4):
                        v.wait_ge(pv_s, base3 + KC)
                        xa.add(v.reciprocal(RL1[:, qsub:qsub + 1],
                                            PV1[qsub][:, 0:1]))
                        xa.w(v)
                    for dh in range(2):
                        for qsub in range(4):
                            v.wait_ge(pv_s, base3 + (dh + 2) * KC)
                            xa.add(v.scalar_tensor_tensor(
                                X16[:, qt * 4 + qsub, dh * 512:(dh + 1) * 512],
                                PV1[qsub][:, 0:512], RL1[:, qsub:qsub + 1],
                                X16[:, qt * 4 + qsub, dh * 512:(dh + 1) * 512],
                                ALU.mult, ALU.add))


def _emit_ffn1(nc, d):
    es = d["es"]
    fsl = es.enter_context(nc.semaphore("fsl"))
    fslb = es.enter_context(nc.semaphore("fslb"))
    fkw = [es.enter_context(nc.semaphore("fka")), es.enter_context(nc.semaphore("fkb"))]
    fvw = [es.enter_context(nc.semaphore("fva")), es.enter_context(nc.semaphore("fvb"))]
    fm_s = es.enter_context(nc.semaphore("ffm"))
    fg_s = es.enter_context(nc.semaphore("ffg"))
    fa_s = es.enter_context(nc.semaphore("ffa"))
    fx_s = es.enter_context(nc.semaphore("ffx"))
    ft_s = es.enter_context(nc.semaphore("fft"))

    XNTS, X16, IDENT, ONESF = d["XNTS"], d["X16"], d["IDENT"], d["ONESF"]
    GCH = 11

    fes = ExitStack()
    fsb = lambda n, shp, dt=F32: fes.enter_context(nc.sbuf_tensor(n, list(shp), dt))
    fps = lambda n, shp, dt=F32: fes.enter_context(nc.psum_tensor(n, list(shp), dt))
    with fes:
        SELTC = fsb("SELTC", (128, 8, E1), BF16)
        RSCS = fsb("RSCS", (1, 1))
        RSCB = fsb("RSCB", (128, 1))
        SP16 = fsb("SP16", (128, E1))
        TOP8F = fsb("TOP8F", (128, 8))
        MAXBF = fsb("MAXBF", (128, 8))
        ZAPF = fsb("ZAPF", (128, E1))
        MSKF = fsb("MSKF", (128, E1))
        SUM4 = fsb("SUM4", (128, 2))
        G16LS = fsb("G16LS", (128, NTILE, E1))
        GT16 = fsb("GT16", (16, NT), BF16)
        SELC1 = fsb("SELC1", (16, E1, 128), BF16)
        GEB1S = fsb("GEB1S", (128, E1, 512), BF16)
        HJT = fsb("HJT", (128, 32, 512), BF16)
        KEYC = fsb("KEYC", (128, 2, 8, 128), BF16)
        VALC = fsb("VALC", (128, 2, 512), BF16)
        f12 = ExitStack()
        f12ps = lambda n, shp, dt=F32: f12.enter_context(nc.psum_tensor(n, list(shp), dt))
        f12.__enter__()
        SELP = f12ps("SELP", (128, E1))
        RSB = f12ps("RSB", (128, 1))
        GTP = f12ps("GTP", (16, 128))

        with nc.Block() as blk:
            @blk.gpsimd
            def _(g):
                fl = Ch(fsl)
                fl.add(g.dma_start(SELTC[:], d["selt"].rearrange("(c p) e -> p c e", p=128)), 16)
                fl.add(g.dma_start(RSCS[:], d["rsc"][:, :]), 16)
                g.memset(SELC1[:], 1.0).then_inc(fslb, 1)
                g.wait_ge(fslb, 1)
                g.affine_select(out=SELC1[:], in_=SELC1[:], compare_op=ALU.is_ge,
                                fill=0.0, base=0, pattern=[[-1, E1], [0, 128]],
                                channel_multiplier=1).then_inc(fslb, 1)
                g.wait_ge(fslb, 2)
                g.affine_select(out=SELC1[:], in_=SELC1[:], compare_op=ALU.is_ge,
                                fill=0.0, base=0, pattern=[[1, E1], [0, 128]],
                                channel_multiplier=-1).then_inc(fslb, 1)

            @blk.tensor
            def _(t):
                fm = Ch(fm_s)
                t.wait_ge(fsl, 32)
                fm.add(t.matmul(RSB[:], ONESF[:], RSCS[:], start=True, stop=True))
                for n in range(NTILE):
                    if n >= 1:
                        t.wait_ge(fa_s, n)
                    inst = None
                    for c in range(8):
                        inst = t.matmul(SELP[:], XNTS[:, c, n * 128:(n + 1) * 128],
                                        SELTC[:, c, :], start=(c == 0), stop=(c == 7))
                    fm.add(inst)

            @blk.scalar
            def _(s):
                fa = Ch(fa_s)
                for n in range(NTILE):
                    s.wait_ge(fm_s, n + 2)
                    if n >= 1:
                        s.wait_ge(fg_s, 1 + GCH * n)
                    fa.add(s.activation(SP16[:], SELP[:], AF.Sigmoid))

            @blk.vector
            def _(v):
                fg = Ch(fg_s)
                v.wait_ge(fm_s, 1)
                fg.add(v.tensor_copy(RSCB[:], RSB[:]))
                for n in range(NTILE):
                    v.wait_ge(fa_s, n + 1)
                    fg.add(v.max(out=TOP8F[:], in_=SP16[:])); fg.w(v)
                    fg.add(v.memset(MAXBF[:], -1e30)); fg.w(v)
                    fg.add(v.tensor_copy(MAXBF[:, 0:4], TOP8F[:, 0:4])); fg.w(v)
                    fg.add(v.match_replace(out=ZAPF[:], in_to_replace=MAXBF[:],
                                           in_values=SP16[:], imm_value=-5.0)); fg.w(v)
                    fg.add(v.tensor_tensor(MSKF[:], SP16[:], ZAPF[:], ALU.not_equal)); fg.w(v)
                    fg.add(v.reduce_sum(SUM4[:, 0:1], TOP8F[:, 0:4], AX.X)); fg.w(v)
                    fg.add(v.tensor_scalar(SUM4[:, 0:1], SUM4[:, 0:1], 1e-9, None,
                                           ALU.max)); fg.w(v)
                    fg.add(v.reciprocal(SUM4[:, 1:2], SUM4[:, 0:1])); fg.w(v)
                    fg.add(v.tensor_tensor(SUM4[:, 1:2], SUM4[:, 1:2], RSCB[:], ALU.mult)); fg.w(v)
                    fg.add(v.tensor_scalar(G16LS[:, n, :], SP16[:], SUM4[:, 1:2], None,
                                           ALU.mult)); fg.w(v)
                    fg.add(v.tensor_tensor(G16LS[:, n, :], G16LS[:, n, :], MSKF[:],
                                           ALU.mult))

        with nc.Block() as blk:
            @blk.tensor
            def _(t):
                ft = Ch(ft_s)
                for n in range(NTILE):
                    if n >= 1:
                        t.wait_ge(fa_s, NTILE + n)
                    ft.add(t.transpose(GTP[:], G16LS[:, n, :], IDENT[:]))

            @blk.scalar
            def _(s):
                fa = Ch(fa_s)
                fa.n = NTILE
                for n in range(NTILE):
                    s.wait_ge(ft_s, n + 1)
                    fa.add(s.copy(GT16[0:16, n * 128:(n + 1) * 128], GTP[:]))

        f12.close()
        GBP = fps("GBP", (128, 512))
        HJP = [fps("HJP0", (128, 512)), fps("HJP1", (128, 512))]
        YV = fps("YV", (128, 4, 512))

        pe_units = []
        for tq in range(4):
            pe_units += [("geb", tq, e) for e in range(E1)]
            pe_units += [("key", tq, ejc) for ejc in range(32)]
            for dh in range(2):
                pe_units += [("val", tq, dh, ejc) for ejc in range(32)]
        fm_pos = {u: i + 1 for i, u in enumerate(pe_units)}

        def fx_geb(idx):
            return 56 * (idx // 16) + idx % 16 + 1

        def fx_hj(kg):
            return 56 * (kg // 32) + 16 + kg % 32 + 1

        def fx_adds_done(q):
            tq, dh = divmod(q, 2)
            return 56 * tq + 48 + 4 * (dh + 1)

        with nc.Block() as blk:
            @blk.gpsimd
            def _(g):
                for tq in range(4):
                    for ejc in range(32):
                        kg = tq * 32 + ejc
                        if kg >= 2:
                            ptq, pj = divmod(kg - 2, 32)
                            g.wait_ge(fm_s, 17 + fm_pos[("key", ptq, pj)])
                        g.dma_start(KEYC[:, kg % 2, :, :],
                                    d["keyst"][:, ejc * 128:(ejc + 1) * 128]
                                    .rearrange("(c p) j -> p c j", p=128)
                                    ).then_inc(fkw[kg % 2], 16)
                    for dh in range(2):
                        for ejc in range(32):
                            vg = (tq * 2 + dh) * 32 + ejc
                            if vg >= 2:
                                pq, pj = divmod(vg - 2, 32)
                                ptq, pdh = divmod(pq, 2)
                                g.wait_ge(fm_s, 17 + fm_pos[("val", ptq, pdh, pj)])
                            g.dma_start(VALC[:, vg % 2, :],
                                        d["valsc"][ejc * 128:(ejc + 1) * 128,
                                                   dh * 512:(dh + 1) * 512]
                                        ).then_inc(fvw[vg % 2], 16)

            @blk.tensor
            def _(t):
                fm = Ch(fm_s)
                fm.n = NTILE + 1
                for tq in range(4):
                    for e in range(E1):
                        idx = tq * E1 + e
                        if idx >= 1:
                            t.wait_ge(fx_s, fx_geb(idx - 1))
                        fm.add(t.matmul(GBP[:], SELC1[:, e, :],
                                        GT16[:, tq * 512:(tq + 1) * 512],
                                        start=True, stop=True))
                    for ejc in range(32):
                        kg = tq * 32 + ejc
                        t.wait_ge(fkw[kg % 2], 16 * (kg // 2 + 1))
                        if kg >= 2:
                            t.wait_ge(fx_s, fx_hj(kg - 2))
                        inst = None
                        for c in range(8):
                            inst = t.matmul(HJP[kg % 2][:], KEYC[:, kg % 2, c, :],
                                            XNTS[:, c, tq * 512:(tq + 1) * 512],
                                            start=(c == 0), stop=(c == 7))
                        fm.add(inst)
                    for dh in range(2):
                        t.wait_ge(fx_s, fx_hj(tq * 32 + 31))
                        if tq * 2 + dh >= 1:
                            t.wait_ge(fx_s, fx_adds_done(tq * 2 + dh - 1))
                        for ejc in range(32):
                            vg = (tq * 2 + dh) * 32 + ejc
                            t.wait_ge(fvw[vg % 2], 16 * (vg // 2 + 1))
                            for tsub in range(4):
                                inst = t.matmul(YV[:, tsub, :],
                                                HJT[:, ejc, tsub * 128:(tsub + 1) * 128],
                                                VALC[:, vg % 2, :],
                                                start=(ejc == 0), stop=(ejc == 31))
                            fm.add(inst)

            @blk.vector
            def _(v):
                fx = Ch(fx_s)
                base = NTILE + 1
                for tq in range(4):
                    for e in range(E1):
                        v.wait_ge(fm_s, base + fm_pos[("geb", tq, e)])
                        fx.add(v.tensor_copy(GEB1S[:, e, :], GBP[:]))
                    for ejc in range(32):
                        kg = tq * 32 + ejc
                        v.wait_ge(fm_s, base + fm_pos[("key", tq, ejc)])
                        if tq >= 1 and ejc == 0:
                            v.wait_ge(fm_s, base + fm_pos[("val", tq - 1, 1, 31)])
                        fx.add(v.scalar_tensor_tensor(HJT[:, ejc, :], HJP[kg % 2][:],
                                                      0.0, GEB1S[:, ejc // 2, :],
                                                      ALU.max, ALU.mult))
                    for dh in range(2):
                        v.wait_ge(fm_s, base + fm_pos[("val", tq, dh, 31)])
                        for tsub in range(4):
                            fx.add(v.tensor_tensor(
                                X16[:, tq * 4 + tsub, dh * 512:(dh + 1) * 512],
                                YV[:, tsub, :],
                                X16[:, tq * 4 + tsub, dh * 512:(dh + 1) * 512],
                                ALU.add))


def _emit_store_c(nc, d):
    od = d["es"].enter_context(nc.semaphore("cods"))
    with nc.Block() as blk:
        @blk.sync
        def _(s):
            s.dma_start(d["xo"].rearrange("(n p) dd -> p n dd", p=128),
                        d["X16"][:]).then_inc(od, 16)
            s.wait_ge(od, 16)



def _core_slices(c):
    b, h = divmod(c, 2)
    return b, RANGES[h]


def _shard_tokens(x):
    out = []
    for c in range(8):
        b, rr = _core_slices(c)
        out.append(np.ascontiguousarray(
            np.concatenate([x[b, s:e] for (s, e) in rr], axis=0)))
    return out


def _unshard_tokens(shards, dtype=np.float32):
    out = np.empty((B, T, D), dtype=dtype)
    for c in range(8):
        b, rr = _core_slices(c)
        off = 0
        for (s, e) in rr:
            out[b, s:e] = shards[c][off:off + (e - s)]
            off += e - s
    return out


_NC_CACHE = {}


def _get_nc(name, builder):
    if name not in _NC_CACHE:
        _NC_CACHE[name] = builder()
    return _NC_CACHE[name]


def kernel(**inputs):
    inp = {k: np.asarray(v) for k, v in inputs.items()}
    x0 = np.ascontiguousarray(inp["L0_attn_x"], dtype=np.float32)
    xs = _shard_tokens(x0)

    nc_a = _get_nc("a", build_launch_a)
    projwt = np.ascontiguousarray(inp["L0_ffn_proj_w"].T, np.float32)
    projb = np.ascontiguousarray(inp["L0_ffn_proj_b"].reshape(1, E0), np.float32)
    ebias = np.ascontiguousarray(inp["L0_ffn_expert_bias"].reshape(1, E0), np.float32)
    res_a = run_bass_kernel_spmd(
        nc_a, [{"xa": xs[c], "projwt": projwt, "projb": projb, "ebias": ebias}
               for c in range(8)], core_ids=list(range(8)))
    xnt = [np.asarray(res_a.results[c]["xnt"]) for c in range(8)]
    w8 = [np.asarray(res_a.results[c]["w8"]) for c in range(8)]

    nc_b = _get_nc("b", build_launch_b)
    W8full = np.concatenate(w8, axis=0)
    w8ts = np.ascontiguousarray(
        W8full.reshape(16, 1024, E0).transpose(2, 0, 1).reshape(128, 1024), np.float32)
    xntn = [np.ascontiguousarray(np.concatenate(
        [xnt[2 * b][:, 0:1024], xnt[2 * b + 1], xnt[2 * b][:, 1024:2048]], axis=1))
        for b in range(4)]
    common_b = {
        "w8ts": w8ts,
        "wqt": np.ascontiguousarray(inp["L0_attn_Wq"].T, np.float32),
        "wkt": np.ascontiguousarray(inp["L0_attn_Wk"].T, np.float32),
        "wvt": np.ascontiguousarray(inp["L0_attn_Wv"].T, np.float32),
        "bq": inp["L0_attn_bq"].reshape(HD, 1).astype(np.float32),
        "bk": inp["L0_attn_bk"].reshape(HD, 1).astype(np.float32),
        "bvr": inp["L0_attn_bv"].reshape(1, HD).astype(np.float32),
        "wl": inp["L0_attn_wavelengths"].reshape(HD, 1).astype(np.float32),
        "pbi": inp["L0_attn_phase_bias"].reshape(HD, 1).astype(np.float32),
        "wot": np.ascontiguousarray(inp["L0_attn_Wo"].T, np.float32),
        "bo": inp["L0_attn_bo"].reshape(1, D).astype(np.float32),
        "sh1t": np.ascontiguousarray(inp["L0_ffn_sh1_w"].T, np.float32),
        "sh1b": np.ascontiguousarray(inp["L0_ffn_sh1_b"].reshape(32, 128).T, np.float32),
        "sh2t": np.ascontiguousarray(inp["L0_ffn_sh2_w"].T, np.float32),
        "sh2b": inp["L0_ffn_sh2_b"].reshape(1, D).astype(np.float32),
        "fc1t": np.ascontiguousarray(np.transpose(inp["L0_ffn_fc1_w"], (0, 2, 1)), np.float32),
        "fc1b": np.ascontiguousarray(
            np.transpose(inp["L0_ffn_fc1_b"].reshape(E0, EH // 128, 128), (0, 2, 1)), np.float32),
        "fc2t": np.ascontiguousarray(np.transpose(inp["L0_ffn_fc2_w"], (0, 2, 1)), np.float32),
        "fc2ba": inp["L0_ffn_fc2_b"].astype(np.float32),
    }
    res_b = run_bass_kernel_spmd(
        nc_b, [dict(common_b, xb=xs[c], xnts=np.ascontiguousarray(xnt[c]),
                    xntn=xntn[c // 2], w8own=np.ascontiguousarray(w8[c]))
               for c in range(8)], core_ids=list(range(8)))
    x1 = [np.asarray(res_b.results[c]["x1"]) for c in range(8)]
    xnt1 = [np.asarray(res_b.results[c]["xnt1"]) for c in range(8)]

    nc_c = _get_nc("c", build_launch_c)
    xntn1 = [np.ascontiguousarray(np.concatenate(
        [xnt1[2 * b][:, 0:1024], xnt1[2 * b + 1], xnt1[2 * b][:, 1024:2048]], axis=1))
        for b in range(4)]
    half = D // 2
    ii = np.arange(half, dtype=np.float64)
    theta = 10000.0 ** (-2.0 * ii / D)
    ang = np.arange(T, dtype=np.float64)[:, None] * theta[None, :]
    cos_t = np.cos(ang).astype(np.float32)
    sin_t = np.sin(ang).astype(np.float32)
    perm = np.concatenate([np.arange(0, D, 2), np.arange(1, D, 2)])
    common_c = {
        "wqtp": np.ascontiguousarray(inp["L1_attn_Wq"][perm].T, np.float32),
        "wktp": np.ascontiguousarray(inp["L1_attn_Wk"][perm].T, np.float32),
        "wvt1": np.ascontiguousarray(inp["L1_attn_Wv"].T, np.float32),
        "costk": np.ascontiguousarray(cos_t.T),
        "sintk": np.ascontiguousarray(sin_t.T),
        "selt": np.ascontiguousarray(inp["L1_ffn_expert_sel"].T, np.float32),
        "keyst": np.ascontiguousarray(
            np.transpose(inp["L1_ffn_keys"], (1, 0, 2)).reshape(D, E1 * J1), np.float32),
        "valsc": np.ascontiguousarray(inp["L1_ffn_values"].reshape(E1 * J1, D), np.float32),
        "rsc": inp["L1_ffn_route_scale"].reshape(1, 1).astype(np.float32),
    }

    def qpos_of(c):
        _, rr = _core_slices(c)
        return np.concatenate([np.arange(s, e) for (s, e) in rr]).astype(np.float32)

    in_c = []
    for c in range(8):
        qp = qpos_of(c)
        in_c.append(dict(
            common_c, xc=x1[c], xnts=np.ascontiguousarray(xnt1[c]),
            xntn=xntn1[c // 2], qpos=qp.reshape(1, NT),
            costq=np.ascontiguousarray(cos_t[qp.astype(np.int64)].T),
            sintq=np.ascontiguousarray(sin_t[qp.astype(np.int64)].T)))
    res_c = run_bass_kernel_spmd(nc_c, in_c, core_ids=list(range(8)))
    outs = [np.asarray(res_c.results[c]["xo"]) for c in range(8)]
    return _unshard_tokens(outs)
